# version 08
# baseline (speedup 1.0000x reference)
import contextlib
import numpy as np
import ml_dtypes
import concourse.bass as bass
import concourse.mybir as mybir
from concourse.bass_utils import run_bass_kernel_spmd

F32 = mybir.dt.float32
BF16 = mybir.dt.bfloat16
ALU = mybir.AluOpType
AF = mybir.ActivationFunctionType
BF = ml_dtypes.bfloat16

D = 1024
DI = 2048
DS = 16
NKD = D // 128
NKE = DI // 128
EPS_OUT = 1e-5
EPS_IN = 1e-12
SLOPE = 0.01


class Sched:
    CSEM = {"v": "V", "s": "S", "t": "T", "g": "G"}

    def __init__(self):
        self.prog = {e: [] for e in ("v", "s", "t", "g", "y")}
        self.cnt = {s: 0 for s in ("V", "S", "T", "G", "C")}
        self.chan_q = {}
        self.waited = {e: {} for e in self.prog}
        self.dead = False

    def _waits(self, eng, deps):
        best = {}
        own = self.CSEM.get(eng)
        for d in deps:
            if not d:
                continue
            s, val = d
            if val <= 0 or s == own:
                continue
            if val > best.get(s, 0):
                best[s] = val
        out = []
        for s, val in best.items():
            if self.waited[eng].get(s, 0) >= val:
                continue
            self.waited[eng][s] = val
            out.append((s, val))
        return out

    def comp(self, eng, fn, deps=()):
        if self.dead:
            return ("V", 0)
        sem = self.CSEM[eng]
        waits = self._waits(eng, deps)
        self.cnt[sem] += 1
        val = self.cnt[sem]

        def run(e, sems):
            for s, v in waits:
                e.wait_ge(sems[s], v)
            fn(e, sems).then_inc(sems[sem], 1)
        self.prog[eng].append(run)
        return (sem, val)

    def dma(self, eng, chan, pairs, deps=()):
        if self.dead:
            return ("V", 0)
        assert eng in ("y", "g")
        assert self.chan_q.setdefault(chan, eng) == eng, chan
        if chan not in self.cnt:
            self.cnt[chan] = 0
        before = self.cnt[chan]
        waits = self._waits(eng, list(deps) + [(chan, before)])
        self.cnt[chan] = before + 16 * len(pairs)
        val = self.cnt[chan]

        def run(e, sems):
            for s, v in waits:
                e.wait_ge(sems[s], v)
            for o, i in pairs:
                e.dma_start(o, i).then_inc(sems[chan], 16)
        self.prog[eng].append(run)
        return (chan, val)

    def bar(self):
        return [(s, self.cnt[s]) for s in self.cnt]


def build(T=2048, groups=None, stop_after=None, DEBUG_HALO_ZERO=False):
    if groups is None:
        groups = [[0, 1, 2, 3], [4, 5, 6, 7]]
    TH = T + 4
    TT_ = 512
    NT = T // TT_
    assert T % TT_ == 0

    nc = bass.Bass("TRN2", target_bir_lowering=False, detect_race_conditions=False)

    def inp(name, shape, dt=F32):
        return nc.dram_tensor(name, list(shape), dt, kind="ExternalInput").ap()

    x_h = inp("x", [D, TH])
    wq_h = inp("wq_t", [D, D])
    rt_h = inp("rt_t", [D, 256])
    lv_h = inp("lv", [256, D])
    wtout_h = inp("wout_t", [D, D])
    mtree_h = inp("mtree", [512, 256])
    wxs_h = inp("inproj_xs_t", [D, DI])
    wz_h = inp("inproj_z_t", [D, DI])
    xp_h = inp("xproj_t", [DI, 128])
    dtp_h = inp("dtproj_t", [64, DI])
    wop_h = inp("outproj_t", [DI, D])
    w1_h = inp("w1_t", [D, DI])
    w2_h = inp("w2_t", [DI, D])
    alog_h = inp("alog_c", [128, 256])
    d_h = inp("d_c", [128, 16])
    convw_h = inp("convw_c", [128, 64])
    convb_h = inp("convb_c", [128, 16])
    dtb_h = inp("dtb_c", [128, 16])
    b1_h = inp("b1_c", [128, 16])
    b2_h = inp("b2_c", [128, 8])
    woutb_h = inp("woutb_c", [128, 8])
    ln1w_h = inp("ln1w_c", [128, 8])
    ln1b_h = inp("ln1b_c", [128, 8])
    ln2w_h = inp("ln2w_c", [128, 8])
    ln2b_h = inp("ln2b_c", [128, 8])
    onesbf_h = inp("ones_od_bf", [128, 1], BF16)
    onesrow_h = inp("ones_row", [1, 128])
    ident_h = inp("ident_bf", [128, 128], BF16)
    hmask_h = inp("halo_mask", [128, 1])
    hmask3_h = inp("halo_mask3", [128, 4], mybir.dt.uint8)
    uw_h = inp("uw_c", [128, 3])
    uwc_h = inp("uwc_c", [128, 3])

    out_h = nc.dram_tensor("out", [D, T], F32, kind="ExternalOutput").ap()

    sc_x1 = nc.dram_tensor("sc_x1", [D, TH], F32).ap()
    sc_x2 = nc.dram_tensor("sc_x2", [D, T], F32).ap()
    sc_h = nc.dram_tensor("sc_h", [D, T], F32).ap()
    sc_dt = nc.dram_tensor("sc_dt", [DI, T], F32).ap()
    sc_ds = nc.dram_tensor("sc_ds", [DI, T], F32).ap()
    sc_xs = nc.dram_tensor("sc_xs", [DI, T], BF16).ap()
    sc_y = nc.dram_tensor("sc_y", [DI, T], BF16).ap()
    sc_rows = nc.dram_tensor("sc_rows", [2, TH], F32).ap()
    cc_in_t = nc.dram_tensor("cc_in", [128, 512], F32)
    cc_out_t = nc.dram_tensor("cc_out", [512, 512], F32)

    S = Sched()

    with contextlib.ExitStack() as ctx:
        E = ctx.enter_context

        AB_N = 50176
        AF_N = 13312
        ab = E(nc.sbuf_tensor("AB", [128, AB_N], BF16))
        af = E(nc.sbuf_tensor("AF", [128, AF_N], F32))

        MB = {
            "wq": (0, 8192), "rt": (8192, 2048), "lv": (10240, 2048),
            "wtout": (12288, 8192), "fxn": (20480, 4096), "fq": (24576, 4096),
            "flp": (28672, 1024), "fsq": (29696, 1024), "fxb": (30720, 1024),
            "m1xn": (0, 4096), "m1sil": (4096, 8192), "m1pre": (12288, 8256),
            "m1dtr": (20544, 512), "m1sq": (21056, 1024), "m1xb": (22080, 1024),
            "dtp": (46080, 2048), "xpw": (48128, 2048),
            "bb": (0, 16384), "cb": (16384, 16384),
            "axs": (32768, 2048), "aw": (34816, 2048), "adbx": (36864, 2048),
            "ah": (38912, 2048), "az": (40960, 4096), "ay": (45056, 4096),
            "bebf": (36864, 4096),
            "wz": (0, 16384), "wop": (16384, 16384), "m3yg": (32768, 8192),
            "m3xn2": (40960, 4096), "m3zs": (45056, 1024),
            "m3sq": (46080, 1024), "m3xb": (47104, 1024),
            "w1": (0, 16384), "w2": (16384, 16384), "m4f": (32768, 8192),
            "m4hbf": (40960, 4096), "m4zs": (45056, 1024),
            "m4sq": (46080, 1024), "m4xb": (47104, 1024),
        }
        MF = {
            "xt": (0, 4096), "v4": (4096, 2048), "esp": (6144, 1024),
            "adt": (0, 2048), "ads": (2048, 2048), "ada": (4096, 4096),
            "adt2": (8192, 2048), "ads2": (10240, 2048),
            "t32a": (4096, 4096), "t32b": (8192, 4096), "scr": (12288, 1024),
        }
        for n, (s, c) in MB.items():
            assert s + c <= AB_N, n
        for n, (s, c) in MF.items():
            assert s + c <= AF_N, n

        def B(name, c0=0, c1=None, p0=0, p1=128):
            s, n = MB[name]
            if c1 is None:
                c1 = n
            assert 0 <= c0 <= c1 <= n, (name, c0, c1)
            return ab[p0:p1, s + c0:s + c1]

        def F(name, c0=0, c1=None, p0=0, p1=128):
            s, n = MF[name]
            if c1 is None:
                c1 = n
            assert 0 <= c0 <= c1 <= n, (name, c0, c1)
            return af[p0:p1, s + c0:s + c1]

        ident = E(nc.sbuf_tensor("ident", [128, 128], BF16))
        onesbf = E(nc.sbuf_tensor("onesbf", [128, 1], BF16))
        onesrow = E(nc.sbuf_tensor("onesrow", [1, 128], F32))
        a_sb = E(nc.sbuf_tensor("a_sb", [128, 256], F32))
        alog_sb = E(nc.sbuf_tensor("alog_sb", [128, 256], F32))
        dcol = E(nc.sbuf_tensor("dcol", [128, 16], F32))
        convw = E(nc.sbuf_tensor("convw", [128, 64], F32))
        convb = E(nc.sbuf_tensor("convb", [128, 16], F32))
        dtb = E(nc.sbuf_tensor("dtb", [128, 16], F32))
        b1c = E(nc.sbuf_tensor("b1c", [128, 16], F32))
        b2c = E(nc.sbuf_tensor("b2c", [128, 8], F32))
        woutbh = E(nc.sbuf_tensor("woutbh", [128, 8], F32))
        woutb = E(nc.sbuf_tensor("woutb", [128, 8], F32))
        ln1w = E(nc.sbuf_tensor("ln1w", [128, 8], F32))
        ln1b = E(nc.sbuf_tensor("ln1b", [128, 8], F32))
        ln2w = E(nc.sbuf_tensor("ln2w", [128, 8], F32))
        ln2b = E(nc.sbuf_tensor("ln2b", [128, 8], F32))
        hmask = E(nc.sbuf_tensor("hmask", [128, 1], F32))
        hmask3 = E(nc.sbuf_tensor("hmask3", [128, 4], mybir.dt.uint8))
        zero3 = E(nc.sbuf_tensor("zero3", [128, 4], BF16))
        tail3 = E(nc.sbuf_tensor("tail3", [128, 4], BF16))
        uw = E(nc.sbuf_tensor("uw", [128, 3], F32))
        uwc = E(nc.sbuf_tensor("uwc", [128, 3], F32))
        zcol = E(nc.sbuf_tensor("zcol", [128, 1], F32))
        zerot = E(nc.sbuf_tensor("zerot", [128, 2048], BF16))
        lnrow = E(nc.sbuf_tensor("lnrow", [1, 2048], F32))
        rowio = E(nc.sbuf_tensor("rowio", [1, 1024], F32))
        brow = E(nc.sbuf_tensor("brow", [16, T], BF16))
        crow = E(nc.sbuf_tensor("crow", [16, T], BF16))
        rowtb = E(nc.sbuf_tensor("rowtb", [1, T], BF16))
        rowtc = E(nc.sbuf_tensor("rowtc", [1, T], BF16))
        statec = E(nc.sbuf_tensor("statec", [128, 256], F32))
        pdec = E(nc.sbuf_tensor("pdec", [128, 256], F32))
        dse = E(nc.sbuf_tensor("dse", [128, 16], F32))
        hin = E(nc.sbuf_tensor("hin", [128, 256], F32))
        hint1 = E(nc.sbuf_tensor("hint1", [128, 256], F32))
        gath = E(nc.sbuf_tensor("gath", [128, 2048], F32))
        cscr = E(nc.sbuf_tensor("cscr", [128, 16], F32))
        mtree = E(nc.sbuf_tensor("mtree_sb", [128, 1024], F32))
        xdhi = E(nc.sbuf_tensor("xdhi", [128, 512], BF16))

        ps_mm = E(nc.psum_tensor("ps_mm", [128, 1024], F32))
        ps_rows = E(nc.psum_tensor("ps_rows", [128, 512], F32))
        ps_rows2 = E(nc.psum_tensor("ps_rows2", [128, 512], F32))
        ps_big = E(nc.psum_tensor("ps_big", [128, 2048], F32))

        t_ld0 = S.dma("y", "c_ld0", [
            (ident[:], ident_h), (onesbf[:], onesbf_h), (onesrow[:], onesrow_h),
            (alog_sb[:], alog_h), (dcol[:], d_h), (convw[:], convw_h),
            (convb[:], convb_h), (dtb[:], dtb_h), (b1c[:], b1_h), (b2c[:], b2_h),
            (woutb[:], woutb_h), (ln1w[:], ln1w_h), (ln1b[:], ln1b_h),
            (ln2w[:], ln2w_h), (ln2b[:], ln2b_h), (hmask[:], hmask_h),
            (uw[:], uw_h), (uwc[:], uwc_h),
        ])
        t_mt = S.dma("y", "c_ld0", [(mtree[:, i * 256:(i + 1) * 256],
                            mtree_h[i * 128:(i + 1) * 128, :]) for i in range(4)])
        t_hm3 = S.dma("y", "c_ld0", [(hmask3[:], hmask3_h)])
        t_z3 = S.comp("v", lambda e, m: e.memset(zero3[:], 0.0))
        t_as1 = S.comp("s", lambda e, m: e.activation(a_sb[:], alog_sb[:], AF.Exp), deps=[t_ld0])
        t_asb = S.comp("v", lambda e, m: e.tensor_scalar_mul(a_sb[:], a_sb[:], -1.0), deps=[t_as1])
        t_wbh = S.comp("v", lambda e, m: e.tensor_scalar_mul(woutbh[:], woutb[:], 0.5), deps=[t_ld0])
        t_zc = S.comp("v", lambda e, m: e.memset(zcol[:], 0.0))
        t_zt = S.comp("v", lambda e, m: e.memset(zerot[:], 0.0))

        def zbc(w):
            return bass.AP(zcol, 0, [[1, 128], [0, w]])

        def row_bcast_ap(handle, c0, w):
            sl = handle[0:1, c0:c0 + w]
            pat = [list(p) for p in sl.ap]
            return bass.AP(sl.tensor, sl.offset, [pat[0], [0, 128], pat[1]])

        def wload(name, hbm, kchunks, ocols, deps=()):
            pairs = [(B(name, c0=k * ocols, c1=(k + 1) * ocols),
                      hbm[k * 128:(k + 1) * 128, :]) for k in range(kchunks)]
            return S.dma("g", "g_wt", pairs, deps=deps)

        t_wq = wload("wq", wq_h, NKD, D)
        t_rt = wload("rt", rt_h, NKD, 256)
        t_lv = wload("lv", lv_h, 2, D)
        t_wtout = wload("wtout", wtout_h, NKD, D)
        t_dtp = S.dma("g", "g_wt", [(B("dtp", c0=0, c1=2048, p0=0, p1=64), dtp_h)])
        t_xpw = S.dma("g", "g_wt", [(B("xpw", c0=k * 128, c1=(k + 1) * 128),
                             xp_h[k * 128:(k + 1) * 128, :]) for k in range(NKE)])

        ps_free = [None, None]

        def emit_mm(lhs_fn, rhs_fn, nk, noc, W, evac_fn, deps=(), out_rows=128):
            toks = []
            for oc in range(noc):
                pp = oc % 2
                ps_ap = ps_mm[0:out_rows, pp * 512:pp * 512 + W]
                dps = list(deps) + ([ps_free[pp]] if ps_free[pp] else [])

                def mmg(e, m, oc=oc, ps_ap=ps_ap):
                    last = None
                    for k in range(nk):
                        last = e.matmul(ps_ap, lhs_fn(k, oc), rhs_fn(k),
                                        start=(k == 0), stop=(k == nk - 1))
                    return last
                t_mm = S.comp("t", mmg, deps=dps)
                t_ev = evac_fn(oc, ps_ap, [t_mm])
                ps_free[pp] = t_ev
                toks.append(t_ev)
            return toks

        lng = []

        def emit_ln(src_fn, W, eps, sqn, xbn, dst_fn, gamma=None, beta=None,
                    save_rows=None, deps=()):
            deps = list(deps)
            guard = list(lng)
            t_m1 = t_m2 = None
            for k in range(NKD):
                pp = k % 2
                t_xb = S.comp("g", lambda e, m, k=k, pp=pp: e.tensor_copy(
                    B(xbn, c0=pp * 512, c1=pp * 512 + W), src_fn(k)),
                    deps=deps + [t_m1] + (guard if k < 2 else []))
                t_sq = S.comp("g", lambda e, m, k=k, pp=pp: e.tensor_tensor(
                    B(sqn, c0=pp * 512, c1=pp * 512 + W), src_fn(k), src_fn(k), ALU.mult),
                    deps=deps + [t_m2])
                t_m1 = S.comp("t", lambda e, m, k=k, pp=pp: e.matmul(
                    ps_rows[0:1, 0:W], onesbf[:], B(xbn, c0=pp * 512, c1=pp * 512 + W),
                    start=(k == 0), stop=(k == NKD - 1)),
                    deps=[t_xb] + (guard if k == 0 else []))
                t_m2 = S.comp("t", lambda e, m, k=k, pp=pp: e.matmul(
                    ps_rows2[0:1, 0:W], onesbf[:], B(sqn, c0=pp * 512, c1=pp * 512 + W),
                    start=(k == 0), stop=(k == NKD - 1)),
                    deps=[t_sq] + (guard if k == 0 else []))
            t_mr = S.comp("s", lambda e, m: e.copy(lnrow[0:1, 0:W], ps_rows[0:1, 0:W]),
                          deps=[t_m1, t_m2] + guard)
            t_ms = S.comp("s", lambda e, m: e.activation(
                lnrow[0:1, 512:512 + W], ps_rows[0:1, 0:W], AF.Square), deps=[t_m1, t_m2] + guard)
            t_va = S.comp("v", lambda e, m: e.scalar_tensor_tensor(
                lnrow[0:1, 1024:1024 + W], ps_rows2[0:1, 0:W], float(eps),
                lnrow[0:1, 512:512 + W], ALU.add, ALU.subtract), deps=[t_ms, t_m2] + guard)
            t_mb = S.comp("t", lambda e, m: e.matmul(
                ps_big[:, 0:W], onesrow[:], lnrow[0:1, 0:W], start=True, stop=True),
                deps=[t_mr] + deps + guard)
            t_vb = S.comp("t", lambda e, m: e.matmul(
                ps_big[:, 512:512 + W], onesrow[:], lnrow[0:1, 1024:1024 + W],
                start=True, stop=True), deps=[t_va] + deps + guard)
            t_rc = S.comp("v", lambda e, m: e.reciprocal(
                F("scr", c0=0, c1=W), ps_big[:, 512:512 + W]), deps=[t_vb] + guard)
            t_r = S.comp("s", lambda e, m: e.activation(
                F("scr", c0=512, c1=512 + W), F("scr", c0=0, c1=W), AF.Sqrt),
                deps=[t_rc] + guard)
            rows_tok = None
            if save_rows is not None:
                rows_tok = S.dma("y", "c_rows", [(save_rows[0], lnrow[0:1, 0:W]),
                                                 (save_rows[1], lnrow[0:1, 1024:1024 + W])],
                                 deps=[t_va, t_mr])
            toks = []
            for k in range(NKD):
                dst = dst_fn(k)
                t_n1 = S.comp("v", lambda e, m, k=k, dst=dst: e.tensor_tensor(
                    dst, src_fn(k), ps_big[:, 0:W], ALU.subtract), deps=[t_mb] + deps)
                t_n2 = S.comp("v", lambda e, m, k=k, dst=dst: e.tensor_tensor(
                    dst, dst, F("scr", c0=512, c1=512 + W), ALU.mult), deps=[t_r, t_n1])
                if gamma is not None:
                    t_n2 = S.comp("v", lambda e, m, k=k, dst=dst: e.tensor_scalar(
                        dst, dst, gamma[:, k:k + 1], beta[:, k:k + 1], ALU.mult, ALU.add),
                        deps=[t_n2])
                toks.append(t_n2)
            lng.clear()
            lng.extend(toks + [t_mr, t_ms, t_va, t_mb, t_vb, t_rc, t_r])
            if rows_tok is not None:
                lng.append(rows_tok)
                toks.append(rows_tok)
            return toks

        def emit_ffn(xin_fn, W, out_dst_fn, wtoks, deps=()):
            deps = list(deps)
            toks_ln = emit_ln(xin_fn, W, EPS_OUT, "fsq", "fxb",
                              lambda k: B("fxn", c0=k * 512, c1=k * 512 + W), deps=deps)

            def xn(k):
                return B("fxn", c0=k * 512, c1=k * 512 + W)

            def ev_q(oc, ps_ap, d):
                return S.comp("s", lambda e, m, oc=oc, ps_ap=ps_ap: e.copy(
                    B("fq", c0=oc * 512, c1=oc * 512 + W), ps_ap), deps=d)
            t_q = emit_mm(lambda k, oc: B("wq", c0=k * D + oc * 128, c1=k * D + (oc + 1) * 128),
                          xn, NKD, NKD, W, ev_q, deps=toks_ln + [wtoks["wq"]])

            def q(k):
                return B("fq", c0=k * 512, c1=k * 512 + W)

            def ev_u(oc, ps_ap, d):
                t_e1 = S.comp("s", lambda e, m, ps_ap=ps_ap: e.activation(
                    F("esp", c0=0, c1=W), ps_ap, AF.Exp, scale=-1.0), deps=d)
                t_v1 = S.comp("s", lambda e, m, oc=oc: e.activation(
                    F("v4", c0=oc * 512, c1=oc * 512 + W), F("esp", c0=0, c1=W),
                    AF.Ln, bias=1.0), deps=[t_e1])
                t_e2 = S.comp("s", lambda e, m, ps_ap=ps_ap: e.activation(
                    F("esp", c0=0, c1=W), ps_ap, AF.Exp), deps=[t_v1])
                t_v2 = S.comp("s", lambda e, m, oc=oc: e.activation(
                    F("v4", c0=(2 + oc) * 512, c1=(2 + oc) * 512 + W),
                    F("esp", c0=0, c1=W), AF.Ln, bias=1.0), deps=[t_e2])
                return t_v2
            t_v = emit_mm(lambda k, oc: B("rt", c0=k * 256 + oc * 128, c1=k * 256 + (oc + 1) * 128),
                          q, NKD, 2, W, ev_u, deps=t_q + [wtoks["rt"]])

            def ev_lp(oc, ps_ap, d):
                return S.comp("s", lambda e, m, oc=oc, ps_ap=ps_ap: e.activation(
                    B("flp", c0=oc * 512, c1=oc * 512 + W), ps_ap, AF.Exp, scale=-1.0), deps=d)
            t_lp = emit_mm(lambda k, oc: mtree[:, k * 256 + oc * 128:k * 256 + (oc + 1) * 128],
                           lambda k: F("v4", c0=k * 512, c1=k * 512 + W),
                           4, 2, W, ev_lp, deps=t_v + [t_mt])

            def ev_ret(oc, ps_ap, d):
                return S.comp("s", lambda e, m, oc=oc, ps_ap=ps_ap: e.copy(
                    B("fq", c0=oc * 512, c1=oc * 512 + W), ps_ap), deps=d)
            t_ret = emit_mm(lambda k, oc: B("lv", c0=k * D + oc * 128, c1=k * D + (oc + 1) * 128),
                            lambda k: B("flp", c0=k * 512, c1=k * 512 + W),
                            2, NKD, W, ev_ret, deps=t_lp + [wtoks["lv"]])

            out_toks = []
            scr_rd = {0: None, 1: None}

            def ev_o(oc, ps_ap, d):
                sp = oc % 2
                t_sc = S.comp("s", lambda e, m, oc=oc, ps_ap=ps_ap, sp=sp: e.activation(
                    F("scr", c0=sp * 512, c1=sp * 512 + W), ps_ap, AF.Identity,
                    bias=woutbh[:, oc:oc + 1], scale=0.5),
                    deps=d + [t_wbh, scr_rd[sp]])
                t_ad = S.comp("v", lambda e, m, oc=oc, sp=sp: e.tensor_tensor(
                    F("xt", c0=oc * 512, c1=oc * 512 + W),
                    F("xt", c0=oc * 512, c1=oc * 512 + W),
                    F("scr", c0=sp * 512, c1=sp * 512 + W), ALU.add), deps=[t_sc])
                scr_rd[sp] = t_ad
                t_st = S.dma("y", "c_out", [(out_dst_fn(oc), F("xt", c0=oc * 512, c1=oc * 512 + W))],
                             deps=[t_ad])
                out_toks.append(t_st)
                return t_ad
            emit_mm(lambda k, oc: B("wtout", c0=k * D + oc * 128, c1=k * D + (oc + 1) * 128),
                    lambda k: B("fq", c0=k * 512, c1=k * 512 + W),
                    NKD, NKD, W, ev_o, deps=t_ret + [wtoks["wtout"]])
            return out_toks

        tile_widths = [4] + [TT_] * NT
        tile_starts = [0]
        for w in tile_widths[:-1]:
            tile_starts.append(tile_starts[-1] + w)

        def dump_and_stop(src_dram, cast=False, off=None):
            if off is None:
                try:
                    ncol = src_dram.shape[1]
                except Exception:
                    ncol = T
                off = ncol - T
            prev = {}
            bar0 = S.bar()
            for k in range(NKD):
                for q in range(NT):
                    sl = (slice(k * 128, (k + 1) * 128), slice(off + q * 512, off + (q + 1) * 512))
                    if cast:
                        t_l = S.dma("g", "dbg", [(B("m1xn", c0=0, c1=512),
                                                  src_dram[sl[0], sl[1]])],
                                    deps=bar0 + [prev.get("s")])
                        t_c = S.comp("v", lambda e, m: e.tensor_copy(
                            F("xt", c0=0, c1=512), B("m1xn", c0=0, c1=512)), deps=[t_l, prev.get("s")])
                        srcap = F("xt", c0=0, c1=512)
                        t_s = S.dma("y", "dbgo", [(out_h[k * 128:(k + 1) * 128, q * 512:(q + 1) * 512],
                                                   srcap)], deps=[t_c])
                    else:
                        t_l = S.dma("y", "dbg", [(F("xt", c0=0, c1=512),
                                                  src_dram[sl[0], sl[1]])],
                                    deps=bar0 + [prev.get("s")])
                        t_s = S.dma("y", "dbgo", [(out_h[k * 128:(k + 1) * 128, q * 512:(q + 1) * 512],
                                                   F("xt", c0=0, c1=512))], deps=[t_l])
                    prev["s"] = t_s
            fin_tok = S.bar()
            fin_waits = S._waits("g", fin_tok)

            def fin(e, sems):
                for sname, v in fin_waits:
                    e.wait_ge(sems[sname], v)
            S.prog["g"].append(fin)

        wtoks1 = {"wq": t_wq, "rt": t_rt, "lv": t_lv, "wtout": t_wtout}
        prev_sts = []
        for a0, W in zip(tile_starts, tile_widths):
            t_xl = S.dma("y", "c_x", [(F("xt", c0=k * 512, c1=k * 512 + W),
                                x_h[k * 128:(k + 1) * 128, a0:a0 + W]) for k in range(NKD)],
                         deps=prev_sts)
            prev_sts = emit_ffn(lambda k, W=W: F("xt", c0=k * 512, c1=k * 512 + W), W,
                                lambda oc, a0=a0, W=W: sc_x1[oc * 128:(oc + 1) * 128, a0:a0 + W],
                                wtoks1, deps=[t_xl, t_ld0, t_zc])
        bar_f1 = S.bar()
        if stop_after == "f1":
            dump_and_stop(sc_x1)
            S.dead = True
        if stop_after == "f1h":
            dump_and_stop(sc_x1, off=0)
            S.dead = True

        t_wxs = S.dma("g", "g_wt", [(ab[:, 0 + k * DI:0 + (k + 1) * DI],
                             wxs_h[k * 128:(k + 1) * 128, :]) for k in range(NKD)],
                      deps=bar_f1)
        M1OFF = 16384
        MB["wxs"] = (0, 16384)
        MB["m1xn"] = (M1OFF, 4096)
        MB["m1sil"] = (M1OFF + 4096, 8192)
        MB["m1pre"] = (M1OFF + 12288, 8256)
        MB["m1dtr"] = (M1OFF + 20544, 512)
        MB["m1sq"] = (M1OFF + 21056, 1024)
        MB["m1xb"] = (M1OFF + 22080, 1024)
        assert M1OFF + 23104 <= 46080

        m1_prev = list(bar_f1)
        tail_toks = {ec: None for ec in range(NKE)}
        tail_sel_prev = [None]
        xdhi_prev = [None]
        sil_prev = {}
        spill_prev = {}
        xpmm_prev = None
        dtmm_prev = []
        for ti, (a0, W) in enumerate(zip(tile_starts, tile_widths)):
            is_halo = (ti == 0)
            t_x1l = S.dma("y", "c_x", [(F("xt", c0=k * 512, c1=k * 512 + W),
                                 sc_x1[k * 128:(k + 1) * 128, a0:a0 + W]) for k in range(NKD)],
                          deps=m1_prev)
            toks_ln2 = emit_ln(lambda k, W=W: F("xt", c0=k * 512, c1=k * 512 + W), W,
                               EPS_OUT, "m1sq", "m1xb",
                               lambda k: B("m1xn", c0=k * 512, c1=k * 512 + W),
                               save_rows=(sc_rows[0:1, a0:a0 + W], sc_rows[1:2, a0:a0 + W]),
                               deps=[t_x1l])

            def xn2(k, W=W):
                return B("m1xn", c0=k * 512, c1=k * 512 + W)

            def ev_xs(ec, ps_ap, d, is_halo=is_halo, W=W):
                d = d + [tail_toks[ec]]
                if is_halo:
                    if DEBUG_HALO_ZERO:
                        t2 = S.comp("v", lambda e, m, ec=ec: e.memset(
                            B("m1pre", c0=ec * 516, c1=ec * 516 + 4), 0.0), deps=d)
                        tail_toks[ec] = t2
                        return t2
                    t1 = S.comp("s", lambda e, m, ec=ec, ps_ap=ps_ap: e.copy(
                        tail3[:], ps_ap), deps=d + [tail_sel_prev[0]])
                    t2 = S.comp("v", lambda e, m, ec=ec: e.select(
                        B("m1pre", c0=ec * 516, c1=ec * 516 + 4),
                        hmask3[:], tail3[:], zero3[:]), deps=[t1, t_hm3, t_z3])
                    tail_sel_prev[0] = t2
                    tail_toks[ec] = t2
                    return t2
                return S.comp("s", lambda e, m, ec=ec, ps_ap=ps_ap: e.copy(
                    B("m1pre", c0=ec * 516 + 4, c1=ec * 516 + 4 + W), ps_ap), deps=d)
            t_xsp = emit_mm(lambda k, ec: B("wxs", c0=k * DI + ec * 128, c1=k * DI + (ec + 1) * 128),
                            xn2, NKD, NKE, W, ev_xs, deps=toks_ln2 + [t_wxs])
            if is_halo:
                if stop_after == "halo0":
                    fin_tok0 = S.bar()
                    fin_waits0 = S._waits("g", fin_tok0)

                    def fin0(e, sems):
                        for sname, v in fin_waits0:
                            e.wait_ge(sems[sname], v)
                    S.prog["g"].append(fin0)
                    S.dead = True
                if stop_after == "halo":
                    bar0 = S.bar()
                    prevk = []
                    for k in range(NKD):
                        t1 = S.comp("v", lambda e, m, k=k: e.tensor_copy(
                            F("scr", c0=0, c1=3), B("m1pre", c0=(8 + k) * 516, c1=(8 + k) * 516 + 3)), deps=bar0 + prevk)
                        t2 = S.comp("v", lambda e, m, k=k: e.tensor_copy(
                            F("scr", c0=16, c1=19), B("m1xn", c0=k * 512, c1=k * 512 + 3)), deps=bar0)
                        t3 = S.comp("v", lambda e, m, k=k: e.tensor_copy(
                            F("scr", c0=32, c1=35), B("m1pre", c0=k * 516, c1=k * 516 + 3)), deps=bar0)
                        t3b = S.comp("v", lambda e, m, k=k: e.tensor_copy(
                            F("scr", c0=48, c1=51), F("xt", c0=k * 512, c1=k * 512 + 3)), deps=bar0)
                        t4 = S.dma("y", "dbgy", [
                            (out_h[k * 128:(k + 1) * 128, 0:512], F("scr", c0=0, c1=512)),
                        ], deps=[t1, t2, t3, t3b])
                        prevk = [t4]
                    fin_tok = S.bar()
                    fin_waits = S._waits("g", fin_tok)

                    def fin(e, sems):
                        for sname, v in fin_waits:
                            e.wait_ge(sems[sname], v)
                    S.prog["g"].append(fin)
                    S.dead = True
                m1_prev = t_xsp
                continue

            t_sil = []
            new_tails = {}
            for ec in range(NKE):
                sp2 = ec % 2

                def conv(e, m, ec=ec, W=W, sp2=sp2):
                    sp = MB["m1pre"][0] + ec * 516 + 1
                    e.tensor_scalar_mul(F("scr", c0=sp2 * 512, c1=sp2 * 512 + W),
                                        ab[:, sp:sp + W],
                                        convw[:, ec * 4:ec * 4 + 1])
                    last = None
                    for w in range(1, 4):
                        last = e.scalar_tensor_tensor(
                            F("scr", c0=sp2 * 512, c1=sp2 * 512 + W), ab[:, sp + w:sp + w + W],
                            convw[:, ec * 4 + w:ec * 4 + w + 1],
                            F("scr", c0=sp2 * 512, c1=sp2 * 512 + W), ALU.mult, ALU.add)
                    return last
                t_cv = S.comp("v", conv, deps=[t_xsp[ec], t_ld0, sil_prev.get(ec - 2)])
                t_si = S.comp("s", lambda e, m, ec=ec, W=W, sp2=sp2: e.activation(
                    B("m1sil", c0=ec * 512, c1=ec * 512 + W),
                    F("scr", c0=sp2 * 512, c1=sp2 * 512 + W),
                    AF.Silu, bias=convb[:, ec:ec + 1]),
                    deps=[t_cv, spill_prev.get(ec), xpmm_prev])
                sil_prev[ec] = t_si
                t_sp = S.dma("g", "g_xs", [(sc_xs[ec * 128:(ec + 1) * 128, a0 - 4:a0 - 4 + W],
                                    B("m1sil", c0=ec * 512, c1=ec * 512 + W))], deps=[t_si])
                spill_prev[ec] = t_sp
                t_tl = S.comp("v", lambda e, m, ec=ec, W=W: e.tensor_copy(
                    B("m1pre", c0=ec * 516, c1=ec * 516 + 4),
                    B("m1pre", c0=ec * 516 + W, c1=ec * 516 + W + 4)), deps=[t_cv, t_si])
                new_tails[ec] = (t_tl, t_sp)
                t_sil.append(t_si)

            def xsl(k, W=W):
                return B("m1sil", c0=k * 512, c1=k * 512 + W)

            def ev_xd(oc, ps_ap, d, a0=a0, W=W):
                t_d = S.comp("s", lambda e, m, ps_ap=ps_ap: e.copy(
                    B("m1dtr", c0=0, c1=W, p0=0, p1=64), ps_ap[0:64, :]),
                    deps=d + dtmm_prev)
                t_b = S.comp("s", lambda e, m, ps_ap=ps_ap, W=W: e.copy(
                    xdhi[64:80, 0:W], ps_ap[64:80, :]), deps=[t_d, xdhi_prev[0]])
                t_c = S.comp("s", lambda e, m, ps_ap=ps_ap, W=W: e.copy(
                    xdhi[96:112, 0:W], ps_ap[96:112, :]), deps=[t_b])
                t_bd = S.dma("g", "g_bc", [
                    (brow[:, a0 - 4:a0 - 4 + W], xdhi[64:80, 0:W]),
                    (crow[:, a0 - 4:a0 - 4 + W], xdhi[96:112, 0:W])], deps=[t_c])
                xdhi_prev[0] = t_bd
                return t_c
            t_xd = emit_mm(lambda k, oc: B("xpw", c0=k * 128, c1=(k + 1) * 128),
                           xsl, NKE, 1, W, ev_xd,
                           deps=t_sil + [t_xpw, xpmm_prev])
            xpmm_prev = t_xd[0]

            dtmm_prev = []
            m1_prev = []
            esp_spill_prev = {}
            for dc in range(NKE):
                pp = dc % 2

                def mmdt(e, m, dc=dc, pp=pp, W=W):
                    return e.matmul(ps_mm[:, pp * 512:pp * 512 + W],
                                    B("dtp", c0=dc * 128, c1=(dc + 1) * 128, p0=0, p1=64),
                                    B("m1dtr", c0=0, c1=W, p0=0, p1=64),
                                    start=True, stop=True)
                dps = [t_xd[0], t_dtp] + ([ps_free[pp]] if ps_free[pp] else [])
                t_mm = S.comp("t", mmdt, deps=dps)
                dtmm_prev.append(t_mm)
                t_e = S.comp("s", lambda e, m, dc=dc, pp=pp, W=W: e.activation(
                    F("esp", c0=pp * 512, c1=pp * 512 + W),
                    ps_mm[:, pp * 512:pp * 512 + W], AF.Exp, bias=dtb[:, dc:dc + 1]),
                    deps=[t_mm, esp_spill_prev.get(pp)])
                ps_free[pp] = t_e
                t_l = S.comp("s", lambda e, m, pp=pp, W=W: e.activation(
                    F("esp", c0=pp * 512, c1=pp * 512 + W),
                    F("esp", c0=pp * 512, c1=pp * 512 + W), AF.Ln, bias=1.0), deps=[t_e])
                t_sd = S.dma("g", "g_dt", [(sc_dt[dc * 128:(dc + 1) * 128, a0 - 4:a0 - 4 + W],
                                    F("esp", c0=pp * 512, c1=pp * 512 + W))], deps=[t_l])
                esp_spill_prev[pp] = t_sd
                m1_prev.append(t_sd)
            for ec in range(NKE):
                tail_toks[ec] = new_tails[ec][0]
                m1_prev.append(new_tails[ec][1])
            m1_prev.append(t_xd[0])

        bar_m1 = S.bar()
        if stop_after == "m1":
            dump_and_stop(sc_dt)
            S.dead = True
        if stop_after == "m1xs":
            dump_and_stop(sc_xs, cast=True)
            S.dead = True
        if stop_after == "m1xs2":
            dump_and_stop(sc_xs[1024:2048, :], cast=True, off=0)
            S.dead = True

        a_last_y = {}
        ay_store_prev = {}
        scan_guard = []
        t_p_toks = []
        adt_guard = {}
        ads_guard = {}
        for nh in range(2):
            t_bc = []
            for nn in range(8):
                n = nh * 8 + nn
                db1 = bar_m1 if nn < 1 else [t_bc[4 * (nn - 1) + 1]]
                db2 = bar_m1 if nn < 1 else [t_bc[4 * (nn - 1) + 3]]
                t_r1 = S.dma("y", "c_rowt", [(rowtb[0:1, 0:T], brow[n:n + 1, 0:T])],
                             deps=db1 + scan_guard)
                t_b1 = S.dma("y", "c_bb", [(B("bb", c0=nn * T, c1=(nn + 1) * T),
                                    row_bcast_ap(rowtb, 0, T))],
                             deps=[t_r1])
                t_r2 = S.dma("g", "g_rowt", [(rowtc[0:1, 0:T], crow[n:n + 1, 0:T])],
                             deps=db2 + scan_guard)
                t_b2 = S.dma("g", "g_cb", [(B("cb", c0=nn * T, c1=(nn + 1) * T),
                                    row_bcast_ap(rowtc, 0, T))],
                             deps=[t_r2])
                t_bc.extend([t_r1, t_b1, t_r2, t_b2])
            bcast_done = t_bc[-4:] + t_bc[-8:-4]

            prev_dc_y = []
            for dc in range(NKE):
                par = dc % 2
                adtn = "adt" if par == 0 else "adt2"
                adsn = "ads" if par == 0 else "ads2"
                ydeps = a_last_y.get(dc, [])
                t_xsl = S.dma("y", "c_axs", [(B("axs", c0=0, c1=T), sc_xs[dc * 128:(dc + 1) * 128, :])],
                              deps=bar_m1 + scan_guard + prev_dc_y)
                t_dtl = S.dma("y", "c_adt", [(F(adtn, c0=0, c1=T), sc_dt[dc * 128:(dc + 1) * 128, :])],
                              deps=bar_m1 + scan_guard + adt_guard.get(par, []))
                if nh == 0:
                    t_ds = S.comp("v", lambda e, m, adsn=adsn, adtn=adtn: e.tensor_tensor_scan(
                        F(adsn, c0=0, c1=T), F(adtn, c0=0, c1=T), zerot[:, 0:T],
                        0.0, ALU.add, ALU.add),
                        deps=[t_dtl, t_zt] + scan_guard + ads_guard.get(par, [])
                        + (t_p_toks[-1:] if t_p_toks else []))
                    t_dss = S.dma("g", "g_ds", [(sc_ds[dc * 128:(dc + 1) * 128, :], F(adsn, c0=0, c1=T))],
                                  deps=[t_ds])
                    t_p1 = S.comp("g", lambda e, m, dc=dc, adsn=adsn: e.tensor_scalar_mul(
                        cscr[:], a_sb[:, dc * 16:(dc + 1) * 16],
                        F(adsn, c0=T - 1, c1=T)),
                        deps=[t_ds, t_asb] + (t_p_toks[-1:] if t_p_toks else []))
                    t_p2 = S.comp("s", lambda e, m, dc=dc: e.activation(
                        pdec[:, dc * 16:(dc + 1) * 16], cscr[:], AF.Exp), deps=[t_p1])
                    ads_guard[par] = [t_dss, t_p1]
                    t_p_toks.append(t_dss)
                    t_p_toks.append(t_p2)
                    if stop_after == "pd" and dc <= 1:
                        S.dma("g", "g_h", [(sc_h[256:384, dc * 16:(dc + 1) * 16], cscr[:]),
                                           (sc_h[384:512, dc * 16:(dc + 1) * 16],
                                            pdec[:, dc * 16:(dc + 1) * 16])],
                              deps=[t_p2])
                    if stop_after == "pd" and dc == 0:
                        S.dma("g", "g_h", [(sc_h[0:128, 0:T], F(adtn, c0=0, c1=T)),
                                           (sc_h[128:256, 0:T], F(adsn, c0=0, c1=T))],
                              deps=[t_ds])
                t_w = S.comp("v", lambda e, m, adtn=adtn: e.tensor_tensor(
                    B("aw", c0=0, c1=T), F(adtn, c0=0, c1=T), B("axs", c0=0, c1=T), ALU.mult),
                    deps=[t_dtl, t_xsl] + scan_guard)

                toks_sum = []
                z_prev = {}
                t_hl_prev = [None]
                for nn in range(8):
                    n = nh * 8 + nn
                    pr = nn % 2
                    t_da = S.comp("s", lambda e, m, dc=dc, n=n, pr=pr, adtn=adtn: e.activation(
                        F("ada", c0=pr * T, c1=pr * T + T), F(adtn, c0=0, c1=T), AF.Exp,
                        scale=a_sb[:, dc * 16 + n:dc * 16 + n + 1]),
                        deps=[t_dtl, t_asb, z_prev.get(nn - 2)] + scan_guard
                        + (prev_dc_y if nn < 2 else []))
                    t_dbx = S.comp("v", lambda e, m, nn=nn: e.tensor_tensor(
                        B("adbx", c0=0, c1=T), B("aw", c0=0, c1=T), B("bb", c0=nn * T, c1=(nn + 1) * T), ALU.mult),
                        deps=[t_w] + bcast_done)
                    t_h = S.comp("v", lambda e, m, pr=pr: e.tensor_tensor_scan(
                        B("ah", c0=0, c1=T), F("ada", c0=pr * T, c1=pr * T + T), B("adbx", c0=0, c1=T),
                        0.0, ALU.mult, ALU.add), deps=[t_da, t_hl_prev[0]])
                    t_z = S.comp("v", lambda e, m, nn=nn, pr=pr: e.tensor_tensor(
                        B("az", c0=pr * T, c1=pr * T + T), B("ah", c0=0, c1=T),
                        B("cb", c0=nn * T, c1=(nn + 1) * T), ALU.mult),
                        deps=toks_sum[-4:] + (prev_dc_y if nn < 2 else []))
                    z_prev[nn] = t_z
                    t_hl = S.comp("g", lambda e, m, dc=dc, n=n: e.tensor_copy(
                        statec[:, dc * 16 + n:dc * 16 + n + 1], B("ah", c0=T - 1, c1=T)),
                        deps=[t_h])
                    t_hl_prev[0] = t_hl
                    t_p_toks.append(t_hl)
                    for s4 in range(NT):
                        t_sm = S.comp("t", lambda e, m, s4=s4, pr=pr, nn=nn: e.matmul(
                            ps_big[:, s4 * 512:(s4 + 1) * 512], ident[:],
                            B("az", c0=pr * T + s4 * 512, c1=pr * T + (s4 + 1) * 512),
                            start=(nn == 0), stop=(nn == 7)),
                            deps=[t_z] + (ydeps if nn == 0 else []))
                        toks_sum.append(t_sm)
                ys = dc % 2
                y_toks = []
                if nh == 0:
                    for s4 in range(NT):
                        t_ye = S.comp("v", lambda e, m, dc=dc, s4=s4, ys=ys: e.scalar_tensor_tensor(
                            B("ay", c0=ys * T + s4 * 512, c1=ys * T + (s4 + 1) * 512),
                            B("axs", c0=s4 * 512, c1=(s4 + 1) * 512), dcol[:, dc:dc + 1],
                            ps_big[:, s4 * 512:(s4 + 1) * 512], ALU.mult, ALU.add),
                            deps=toks_sum[-NT:] + [t_xsl, ay_store_prev.get(ys)])
                        y_toks.append(t_ye)
                else:
                    t_yl = S.dma("y", "c_yl", [(B("ay", c0=ys * T, c1=ys * T + T),
                                        sc_y[dc * 128:(dc + 1) * 128, :])],
                                 deps=ydeps + bar_m1 + [ay_store_prev.get(ys)])
                    for s4 in range(NT):
                        t_ye = S.comp("v", lambda e, m, s4=s4, ys=ys: e.tensor_tensor(
                            B("ay", c0=ys * T + s4 * 512, c1=ys * T + (s4 + 1) * 512),
                            B("ay", c0=ys * T + s4 * 512, c1=ys * T + (s4 + 1) * 512),
                            ps_big[:, s4 * 512:(s4 + 1) * 512], ALU.add),
                            deps=toks_sum[-NT:] + [t_yl])
                        y_toks.append(t_ye)
                t_yst = S.dma("g", "g_y", [(sc_y[dc * 128:(dc + 1) * 128, :],
                                     B("ay", c0=ys * T, c1=ys * T + T))], deps=y_toks)
                ay_store_prev[ys] = t_yst
                a_last_y[dc] = y_toks + [t_yst]
                prev_dc_y = y_toks
                adt_guard[par] = [t_da, t_w]
            scan_guard = S.bar()

        bar_pa = S.bar()
        if stop_after == "pa":
            dump_and_stop(sc_y, cast=True)
            S.dead = True

        t_pp = list(t_p_toks)

        if stop_after == "pd":
            bar0 = S.bar() + t_pp
            t_c1 = S.comp("v", lambda e, m: e.tensor_copy(F("xt", c0=0, c1=256), pdec[:]), deps=bar0)
            t_c2 = S.comp("v", lambda e, m: e.tensor_copy(F("xt", c0=256, c1=272), cscr[:]), deps=bar0)
            t_c3 = S.comp("v", lambda e, m: e.tensor_copy(F("xt", c0=512, c1=768), statec[:]), deps=bar0)
            S.dma("y", "dbgy", [(out_h[0:128, 0:256], F("xt", c0=0, c1=256)),
                                (out_h[0:128, 256:272], F("xt", c0=256, c1=272)),
                                (out_h[0:128, 512:768], F("xt", c0=512, c1=768)),
                                (out_h[128:256, 0:T], sc_h[0:128, 0:T]),
                                (out_h[256:384, 0:T], sc_h[128:256, 0:T]),
                                (out_h[384:512, 0:64], sc_h[256:384, 0:64]),
                                (out_h[512:640, 0:64], sc_h[384:512, 0:64])],
                  deps=[t_c1, t_c2, t_c3])
            fin_tokp = S.bar()
            fin_waitsp = S._waits("g", fin_tokp)

            def finp(e, sems):
                for sname, v in fin_waitsp:
                    e.wait_ge(sems[sname], v)
            S.prog["g"].append(finp)
            S.dead = True

        cc_in = cc_in_t.ap()
        cc_out = cc_out_t.ap()
        t_ci = S.dma("g", "g_cc", [(cc_in[:, 0:256], statec[:]), (cc_in[:, 256:512], pdec[:])],
                     deps=bar_pa + t_pp)
        def _cc_run(e, m):
            for sme, v in S._waits("g", [t_ci]):
                e.wait_ge(m[sme], v)
            e.collective_compute(
                "AllGather", ALU.bypass, replica_groups=groups,
                ins=[cc_in_t.ap().opt()], outs=[cc_out_t.ap().opt()]).then_inc(m["C"], 1)
        S.prog["g"].append(_cc_run)
        S.cnt["C"] += 1
        t_cc = ("C", S.cnt["C"])
        t_go = S.dma("y", "c_gath", [(gath[:, r * 512:(r + 1) * 512],
                            cc_out[r * 128:(r + 1) * 128, :]) for r in range(4)],
                     deps=[t_cc])
        t_hz = S.comp("v", lambda e, m: e.memset(hin[:], 0.0), deps=bar_pa)
        t_prev = t_hz
        for j in range(3):
            t_1 = S.comp("v", lambda e, m, j=j: e.tensor_tensor(
                hint1[:], gath[:, j * 512 + 256:j * 512 + 512], hin[:], ALU.mult),
                deps=[t_go, t_prev])
            t_2 = S.comp("v", lambda e, m, j=j: e.tensor_tensor(
                hint1[:], hint1[:], gath[:, j * 512:j * 512 + 256], ALU.add), deps=[t_1])
            t_3 = S.comp("v", lambda e, m, j=j: e.tensor_scalar_mul(
                hint1[:], hint1[:], uw[:, j:j + 1]), deps=[t_2])
            t_4 = S.comp("v", lambda e, m, j=j: e.tensor_scalar_mul(
                hin[:], hin[:], uwc[:, j:j + 1]), deps=[t_2])
            t_prev = S.comp("v", lambda e, m: e.tensor_tensor(
                hin[:], hin[:], hint1[:], ALU.add), deps=[t_3, t_4])
        t_hin = t_prev

        pb_guard = []
        b_last_y = {}
        ads_guard_b = {}
        for nh in range(2):
            t_bc = []
            for nn in range(8):
                n = nh * 8 + nn
                dep_b = bar_pa if len(t_bc) < 2 else [t_bc[-1]]
                t_r2 = S.dma("g", "g_rowt", [(rowtc[0:1, 0:T], crow[n:n + 1, 0:T])],
                             deps=dep_b + pb_guard)
                t_b2 = S.dma("g", "g_cb", [(B("cb", c0=nn * T, c1=(nn + 1) * T),
                                    row_bcast_ap(rowtc, 0, T))],
                             deps=[t_r2])
                t_bc.extend([t_r2, t_b2])
            bcast_done = t_bc[-2:]

            prev_dc_y = []
            prev_dc_st = None
            for dc in range(NKE):
                par = dc % 2
                adsn = "ads" if par == 0 else "ads2"
                ydeps = b_last_y.get(dc, [])
                t_dsl = S.dma("y", "c_adt", [(F(adsn, c0=0, c1=T), sc_ds[dc * 128:(dc + 1) * 128, :])],
                              deps=bar_pa + pb_guard + ads_guard_b.get(par, []))
                t_yl = S.dma("y", "c_yl", [(B("ay", c0=0, c1=T), sc_y[dc * 128:(dc + 1) * 128, :])],
                             deps=bar_pa + pb_guard + ydeps + prev_dc_y + [prev_dc_st]
                             + (a_last_y.get(dc, []) if nh == 0 else []))
                toks_sum = []
                c_prev = {}
                for nn in range(8):
                    n = nh * 8 + nn
                    pr = nn % 2
                    t_e = S.comp("s", lambda e, m, dc=dc, n=n, pr=pr, adsn=adsn: e.activation(
                        B("bebf", c0=pr * T, c1=pr * T + T), F(adsn, c0=0, c1=T), AF.Exp,
                        scale=a_sb[:, dc * 16 + n:dc * 16 + n + 1]),
                        deps=[t_dsl, t_asb, c_prev.get(nn - 2)] + pb_guard
                        + (prev_dc_y if nn < 2 else []))
                    t_c = S.comp("v", lambda e, m, dc=dc, n=n, nn=nn, pr=pr: e.scalar_tensor_tensor(
                        B("az", c0=pr * T, c1=pr * T + T),
                        B("bebf", c0=pr * T, c1=pr * T + T),
                        hin[:, dc * 16 + n:dc * 16 + n + 1],
                        B("cb", c0=nn * T, c1=(nn + 1) * T), ALU.mult, ALU.mult),
                        deps=[t_e, t_hin] + toks_sum[-NT:] + bcast_done
                        + (prev_dc_y if nn < 2 else []))
                    c_prev[nn] = t_c
                    for s4 in range(NT):
                        t_sm = S.comp("t", lambda e, m, s4=s4, pr=pr, nn=nn: e.matmul(
                            ps_big[:, s4 * 512:(s4 + 1) * 512], ident[:],
                            B("az", c0=pr * T + s4 * 512, c1=pr * T + (s4 + 1) * 512),
                            start=(nn == 0), stop=(nn == 7)),
                            deps=[t_c] + (ydeps if nn == 0 else []))
                        toks_sum.append(t_sm)
                y_toks = []
                for s4 in range(NT):
                    t_ya = S.comp("v", lambda e, m, s4=s4: e.tensor_tensor(
                        B("ay", c0=s4 * 512, c1=(s4 + 1) * 512),
                        B("ay", c0=s4 * 512, c1=(s4 + 1) * 512),
                        ps_big[:, s4 * 512:(s4 + 1) * 512], ALU.add),
                        deps=toks_sum[-NT:] + [t_yl])
                    y_toks.append(t_ya)
                t_yst = S.dma("g", "g_y", [(sc_y[dc * 128:(dc + 1) * 128, :],
                                     B("ay", c0=0, c1=T))], deps=y_toks)
                b_last_y[dc] = y_toks + [t_yst]
                prev_dc_y = y_toks
                ads_guard_b[par] = [t_e]
                prev_dc_st = t_yst
            pb_guard = S.bar()

        bar_pb = S.bar()
        if stop_after == "pb":
            dump_and_stop(sc_y, cast=True)
            S.dead = True

        t_wz = wload("wz", wz_h, NKD, DI, deps=bar_pa)
        t_wop = wload("wop", wop_h, NKE, D, deps=bar_pb)
        m3_prev = list(bar_pb)
        for si in range(NT):
            a0 = si * TT_
            W = TT_
            t_x1l = S.dma("y", "c_x", [(F("xt", c0=k * 512, c1=k * 512 + W),
                                 sc_x1[k * 128:(k + 1) * 128, 4 + a0:4 + a0 + W])
                                for k in range(NKD)], deps=m3_prev)
            t_ygl = S.dma("y", "c_yl", [(B("m3yg", c0=ec * 512, c1=ec * 512 + W),
                                 sc_y[ec * 128:(ec + 1) * 128, a0:a0 + W])
                                for ec in range(NKE)], deps=m3_prev)
            t_rl = S.dma("y", "c_rows", [(rowio[0:1, 0:W], sc_rows[0:1, 4 + a0:4 + a0 + W]),
                               (rowio[0:1, 512:512 + W], sc_rows[1:2, 4 + a0:4 + a0 + W])],
                         deps=m3_prev)
            t_mb = S.comp("t", lambda e, m, W=W: e.matmul(
                ps_big[:, 0:W], onesrow[:], rowio[0:1, 0:W], start=True, stop=True),
                deps=[t_rl] + m3_prev + list(lng))
            t_rb = S.comp("t", lambda e, m, W=W: e.matmul(
                ps_big[:, 512:512 + W], onesrow[:], rowio[0:1, 512:512 + W],
                start=True, stop=True), deps=[t_rl] + m3_prev + list(lng))
            t_rc2 = S.comp("v", lambda e, m, W=W: e.reciprocal(
                F("scr", c0=0, c1=W), ps_big[:, 512:512 + W]), deps=[t_rb] + m3_prev + list(lng))
            t_rbt = S.comp("s", lambda e, m, W=W: e.activation(
                F("scr", c0=512, c1=512 + W), F("scr", c0=0, c1=W), AF.Sqrt),
                deps=[t_rc2] + m3_prev)
            toks_xn2 = []
            for k in range(NKD):
                t_s1 = S.comp("v", lambda e, m, k=k, W=W: e.tensor_tensor(
                    F("scr", c0=0, c1=W), F("xt", c0=k * 512, c1=k * 512 + W),
                    ps_big[:, 0:W], ALU.subtract), deps=[t_mb, t_x1l, t_rbt])
                t_s2 = S.comp("v", lambda e, m, k=k, W=W: e.tensor_tensor(
                    B("m3xn2", c0=k * 512, c1=k * 512 + W), F("scr", c0=0, c1=W),
                    F("scr", c0=512, c1=512 + W), ALU.mult), deps=[t_rbt, t_s1])
                toks_xn2.append(t_s2)

            def ev_z(ec, ps_ap, d, W=W):
                zp = ec % 2
                t_zs = S.comp("s", lambda e, m, ps_ap=ps_ap, zp=zp: e.activation(
                    B("m3zs", c0=zp * 512, c1=zp * 512 + W), ps_ap, AF.Silu), deps=d)
                return S.comp("v", lambda e, m, ec=ec, zp=zp: e.tensor_tensor(
                    B("m3yg", c0=ec * 512, c1=ec * 512 + W),
                    B("m3yg", c0=ec * 512, c1=ec * 512 + W),
                    B("m3zs", c0=zp * 512, c1=zp * 512 + W), ALU.mult),
                    deps=[t_zs, t_ygl])
            t_zg = emit_mm(lambda k, ec: B("wz", c0=k * DI + ec * 128, c1=k * DI + (ec + 1) * 128),
                           lambda k: B("m3xn2", c0=k * 512, c1=k * 512 + W),
                           NKD, NKE, W, ev_z, deps=toks_xn2 + [t_wz])

            toks_hpre = []

            def ev_op(oc, ps_ap, d, W=W):
                t_s1 = S.comp("v", lambda e, m, oc=oc: e.tensor_tensor(
                    F("scr", c0=0, c1=W), F("xt", c0=oc * 512, c1=oc * 512 + W),
                    ps_big[:, 0:W], ALU.subtract), deps=d + [t_mb])
                t_s2 = S.comp("v", lambda e, m: e.tensor_tensor(
                    F("scr", c0=0, c1=W), F("scr", c0=0, c1=W),
                    F("scr", c0=512, c1=512 + W), ALU.mult), deps=[t_rbt, t_s1])
                t_s3 = S.comp("v", lambda e, m, oc=oc, ps_ap=ps_ap: e.tensor_tensor(
                    F("t32a", c0=oc * 512, c1=oc * 512 + W), ps_ap,
                    F("scr", c0=0, c1=W), ALU.add), deps=[t_s2])
                toks_hpre.append(t_s3)
                return t_s3
            emit_mm(lambda k, oc: B("wop", c0=k * D + oc * 128, c1=k * D + (oc + 1) * 128),
                    lambda k: B("m3yg", c0=k * 512, c1=k * 512 + W),
                    NKE, NKD, W, ev_op, deps=t_zg + [t_wop])

            toks_h = emit_ln(lambda k: F("t32a", c0=k * 512, c1=k * 512 + W), W,
                             EPS_IN, "m3sq", "m3xb",
                             lambda k: F("t32b", c0=k * 512, c1=k * 512 + W),
                             gamma=ln1w, beta=ln1b, deps=toks_hpre)
            m3_prev = []
            for k in range(NKD):
                t_st = S.dma("g", "g_h", [(sc_h[k * 128:(k + 1) * 128, a0:a0 + W],
                                    F("t32b", c0=k * 512, c1=k * 512 + W))],
                             deps=[toks_h[k]])
                m3_prev.append(t_st)
            m3_prev.extend(toks_h)

        bar_m3 = S.bar()
        if stop_after == "m3":
            dump_and_stop(sc_h)
            S.dead = True

        t_w1 = wload("w1", w1_h, NKD, DI, deps=bar_m3)
        t_w2 = wload("w2", w2_h, NKE, D, deps=bar_m3)
        m4_prev = list(bar_m3)
        for si in range(NT):
            a0 = si * TT_
            W = TT_
            t_hl2 = S.dma("y", "c_hl", [(F("t32b", c0=k * 512, c1=k * 512 + W),
                                 sc_h[k * 128:(k + 1) * 128, a0:a0 + W])
                                for k in range(NKD)], deps=m4_prev)
            t_x1l = S.dma("y", "c_x", [(F("xt", c0=k * 512, c1=k * 512 + W),
                                 sc_x1[k * 128:(k + 1) * 128, 4 + a0:4 + a0 + W])
                                for k in range(NKD)], deps=m4_prev)
            toks_hbf = []
            for k in range(NKD):
                t_hb = S.comp("g", lambda e, m, k=k: e.tensor_copy(
                    B("m4hbf", c0=k * 512, c1=k * 512 + W),
                    F("t32b", c0=k * 512, c1=k * 512 + W)), deps=[t_hl2] + m4_prev)
                toks_hbf.append(t_hb)

            def ev_f(ec, ps_ap, d, W=W):
                zp = ec % 2
                t_e1 = S.comp("s", lambda e, m, ec=ec, ps_ap=ps_ap, zp=zp: e.activation(
                    B("m4zs", c0=zp * 512, c1=zp * 512 + W), ps_ap, AF.Identity,
                    bias=b1c[:, ec:ec + 1]), deps=d)
                t_e2 = S.comp("v", lambda e, m, ec=ec, zp=zp: e.tensor_scalar_mul(
                    B("m4f", c0=ec * 512, c1=ec * 512 + W),
                    B("m4zs", c0=zp * 512, c1=zp * 512 + W), SLOPE), deps=[t_e1])
                return S.comp("v", lambda e, m, ec=ec, zp=zp: e.tensor_tensor(
                    B("m4f", c0=ec * 512, c1=ec * 512 + W),
                    B("m4f", c0=ec * 512, c1=ec * 512 + W),
                    B("m4zs", c0=zp * 512, c1=zp * 512 + W), ALU.max), deps=[t_e2])
            t_f = emit_mm(lambda k, ec: B("w1", c0=k * DI + ec * 128, c1=k * DI + (ec + 1) * 128),
                          lambda k: B("m4hbf", c0=k * 512, c1=k * 512 + W),
                          NKD, NKE, W, ev_f, deps=toks_hbf + [t_w1])

            toks_h2 = []

            def ev_h2(oc, ps_ap, d, W=W):
                t_a = S.comp("v", lambda e, m, oc=oc, ps_ap=ps_ap: e.scalar_tensor_tensor(
                    F("t32a", c0=oc * 512, c1=oc * 512 + W), ps_ap, b2c[:, oc:oc + 1],
                    F("t32b", c0=oc * 512, c1=oc * 512 + W), ALU.add, ALU.add), deps=d + [t_hl2])
                toks_h2.append(t_a)
                return t_a
            emit_mm(lambda k, oc: B("w2", c0=k * D + oc * 128, c1=k * D + (oc + 1) * 128),
                    lambda k: B("m4f", c0=k * 512, c1=k * 512 + W),
                    NKE, NKD, W, ev_h2, deps=t_f + [t_w2])

            toks_at = emit_ln(lambda k: F("t32a", c0=k * 512, c1=k * 512 + W), W,
                              EPS_IN, "m4sq", "m4xb",
                              lambda k: F("t32b", c0=k * 512, c1=k * 512 + W),
                              gamma=ln2w, beta=ln2b, deps=toks_h2)
            m4_prev = []
            for k in range(NKD):
                t_x2 = S.comp("v", lambda e, m, k=k: e.tensor_tensor(
                    F("xt", c0=k * 512, c1=k * 512 + W),
                    F("xt", c0=k * 512, c1=k * 512 + W),
                    F("t32b", c0=k * 512, c1=k * 512 + W), ALU.add),
                    deps=[toks_at[k], t_x1l])
                t_st = S.dma("g", "g_x2", [(sc_x2[k * 128:(k + 1) * 128, a0:a0 + W],
                                    F("xt", c0=k * 512, c1=k * 512 + W))], deps=[t_x2])
                m4_prev.append(t_st)

        bar_m4 = S.bar()
        if stop_after == "m4":
            dump_and_stop(sc_x2)
            S.dead = True

        t_wq2 = wload("wq", wq_h, NKD, D, deps=bar_m4)
        t_rt2 = wload("rt", rt_h, NKD, 256, deps=bar_m4)
        t_lv2 = wload("lv", lv_h, 2, D, deps=bar_m4)
        t_wtout2 = wload("wtout", wtout_h, NKD, D, deps=bar_m4)
        wtoks2 = {"wq": t_wq2, "rt": t_rt2, "lv": t_lv2, "wtout": t_wtout2}
        prev_sts = list(bar_m4)
        for si in range(NT):
            a0 = si * TT_
            W = TT_
            t_xl = S.dma("y", "c_x", [(F("xt", c0=k * 512, c1=k * 512 + W),
                                sc_x2[k * 128:(k + 1) * 128, a0:a0 + W]) for k in range(NKD)],
                         deps=prev_sts)
            prev_sts = emit_ffn(lambda k, W=W: F("xt", c0=k * 512, c1=k * 512 + W), W,
                                lambda oc, a0=a0, W=W: out_h[oc * 128:(oc + 1) * 128, a0:a0 + W],
                                wtoks2, deps=[t_xl] + list(bar_m4))

        fin_tok = S.bar()
        fin_waits = S._waits("g", fin_tok)

        def fin(e, sems):
            for s, v in fin_waits:
                e.wait_ge(sems[s], v)
        S.prog["g"].append(fin)

        sem_names = ["V", "S", "T", "G", "C"] + sorted(S.chan_q)
        sems = {n: E(nc.semaphore(n)) for n in sem_names}
        block = E(nc.Block())

        @block.vector
        def _(v):
            for fn in S.prog["v"]:
                fn(v, sems)

        @block.scalar
        def _(s):
            for fn in S.prog["s"]:
                fn(s, sems)

        @block.tensor
        def _(t):
            for fn in S.prog["t"]:
                fn(t, sems)

        @block.gpsimd
        def _(g):
            for fn in S.prog["g"]:
                fn(g, sems)

        @block.sync
        def _(y):
            for fn in S.prog["y"]:
                fn(y, sems)

    return nc


def _tree_matrix():
    M = np.zeros((512, 256), np.float32)
    for j in range(256):
        for lvl in range(8):
            node = (1 << lvl) - 1 + (j >> (8 - lvl))
            right = (j >> (7 - lvl)) & 1
            M[node if right else 256 + node, j] = 1.0
    return M


def _cols(vec, k):
    return np.ascontiguousarray(vec.reshape(k, 128).T.astype(np.float32))


def _xproj_pad(xp):
    out = np.zeros((xp.shape[0], 128), np.float32)
    out[:, 0:80] = xp[:, 0:80]
    out[:, 96:112] = xp[:, 80:96]
    return out


def make_in_maps(inputs, T=2048, n_cores=8):
    f = lambda a: np.ascontiguousarray(np.asarray(a, np.float32))
    x = f(inputs["L0_attn_x"])
    B_, L_, D_ = x.shape
    n_chunks = n_cores // B_
    assert n_chunks * T == L_

    ipw = f(inputs["L0_attn_in_proj_w"])
    rt_t = np.zeros((D, 256), np.float32)
    rt_t[:, :255] = f(inputs["L0_ffn_routers"].T)
    alog = f(inputs["L0_attn_A_log"])
    convw = f(inputs["L0_attn_conv_w"])

    shared = dict(
        wq_t=f(inputs["L0_ffn_Wq"].T),
        rt_t=rt_t,
        lv=f(inputs["L0_ffn_leaf_values"]),
        wout_t=f(inputs["L0_ffn_Wout_w"].T),
        mtree=_tree_matrix(),
        inproj_xs_t=np.ascontiguousarray(ipw[:DI, :].T),
        inproj_z_t=np.ascontiguousarray(ipw[DI:, :].T),
        xproj_t=_xproj_pad(f(inputs["L0_attn_x_proj_w"].T)),
        dtproj_t=f(inputs["L0_attn_dt_proj_w"].T),
        outproj_t=f(inputs["L0_attn_out_proj_w"].T),
        w1_t=f(inputs["L0_attn_ffn_w1"].T),
        w2_t=f(inputs["L0_attn_ffn_w2"].T),
        alog_c=np.ascontiguousarray(alog.reshape(16, 128, 16).transpose(1, 0, 2).reshape(128, 256)),
        convw_c=np.ascontiguousarray(convw.reshape(16, 128, 4).transpose(1, 0, 2).reshape(128, 64)),
        d_c=_cols(f(inputs["L0_attn_D"]), 16),
        convb_c=_cols(f(inputs["L0_attn_conv_b"]), 16),
        dtb_c=_cols(f(inputs["L0_attn_dt_proj_b"]), 16),
        b1_c=_cols(f(inputs["L0_attn_ffn_b1"]), 16),
        b2_c=_cols(f(inputs["L0_attn_ffn_b2"]), 8),
        woutb_c=_cols(f(inputs["L0_ffn_Wout_b"]), 8),
        ln1w_c=_cols(f(inputs["L0_attn_ln1_w"]), 8),
        ln1b_c=_cols(f(inputs["L0_attn_ln1_b"]), 8),
        ln2w_c=_cols(f(inputs["L0_attn_ln2_w"]), 8),
        ln2b_c=_cols(f(inputs["L0_attn_ln2_b"]), 8),
        ones_od_bf=np.full((128, 1), 1.0 / D, BF),
        ones_row=np.ones((1, 128), np.float32),
        ident_bf=np.eye(128, dtype=BF),
    )

    in_maps = []
    for c in range(n_cores):
        seq, ch = divmod(c, n_chunks)
        s0 = ch * T
        xs = np.zeros((D, T + 4), np.float32)
        xs[:, 4:] = x[seq, s0:s0 + T, :].T
        if ch > 0:
            xs[:, 0:4] = x[seq, s0 - 4:s0, :].T
        m = dict(shared)
        m["x"] = np.ascontiguousarray(xs)
        m["halo_mask"] = np.full((128, 1), 0.0 if ch == 0 else 1.0, np.float32)
        m["halo_mask3"] = np.full((128, 4), 0 if ch == 0 else 1, np.uint8)
        uwv = np.zeros((128, 3), np.float32)
        uwv[:, :ch] = 1.0
        m["uw_c"] = uwv
        m["uwc_c"] = np.ascontiguousarray(1.0 - uwv)
        in_maps.append(m)
    return in_maps


def kernel(**inputs):
    T = 2048
    x = np.asarray(inputs["L0_attn_x"], np.float32)
    B_, L_, D_ = x.shape
    in_maps = make_in_maps(inputs, T=T, n_cores=8)
    nc = build(T=T)
    res = run_bass_kernel_spmd(nc, in_maps, core_ids=list(range(8)))
    out = np.empty((B_, L_, D_), np.float32)
    for c in range(8):
        seq, ch = divmod(c, 4)
        out[seq, ch * T:(ch + 1) * T, :] = res.results[c]["out"].T
    return out
